# version 03
# baseline (speedup 1.0000x reference)
import numpy as np
import ml_dtypes
import concourse.bass as bass
import concourse.mybir as mybir
from concourse.bass_utils import run_bass_kernel_spmd

F32 = mybir.dt.float32
BF16 = mybir.dt.bfloat16
AF = mybir.ActivationFunctionType
ALU = mybir.AluOpType
BF = ml_dtypes.bfloat16

B, S, DM, H, HD = 2, 8192, 1024, 16, 64
NC = 8
SL = S // 4
NP = H // 2
NB = S // 128
NQB = SL // 128
NCH = S // 128
QW = 512
NQW = SL // QW
GRP = [3] * 21 + [1]
NGR = len(GRP)
SHIFT = 4.0
EPS = 1e-5
KVC = 257
RBT = NB + NQB
NSLAB = RBT // 8

C_SHIFT, C_SCL, C_ONE, C_ZERO, C_NHALF, C_EPS = 0, 1, 2, 3, 4, 5

ENGS = ("sp", "pe", "act", "dve", "pool")
DEBUG_DUMPS = False
DEBUG_LN1_BYPASS = False
PHASES = "all"
LAST_RES = None


class Prog:

    def __init__(self):
        self.items = {e: [] for e in ENGS}

    def op(self, eng, fn, waits=(), inc=None, flush=False):
        self.items[eng].append((tuple(waits), fn, inc, flush))


def build_nc():
    nc = bass.Bass("TRN2", target_bir_lowering=False)

    d_x = nc.dram_tensor("x", [S, DM], F32, kind="ExternalInput").ap()
    d_xq = nc.dram_tensor("xq", [SL, DM], F32, kind="ExternalInput").ap()
    d_wkv = nc.dram_tensor("wkv", [8, 128, NP * KVC], F32, kind="ExternalInput").ap()
    d_wq = nc.dram_tensor("wq", [8, 128, NP * 128], F32, kind="ExternalInput").ap()
    d_bkv = nc.dram_tensor("bkv", [1, NP * KVC], F32, kind="ExternalInput").ap()
    d_bq = nc.dram_tensor("bq", [1, NP * 128], F32, kind="ExternalInput").ap()
    d_csk = nc.dram_tensor("csk", [128, 2 * NB * 64], F32, kind="ExternalInput").ap()
    d_csq = nc.dram_tensor("csq", [128, 2 * NQB * 64], F32, kind="ExternalInput").ap()
    d_cb = nc.dram_tensor("cbv", [128, 8], F32, kind="ExternalInput").ap()
    d_eyeb = nc.dram_tensor("eyeb", [128, 128], BF16, kind="ExternalInput").ap()
    d_eyef = nc.dram_tensor("eyef", [128, 128], F32, kind="ExternalInput").ap()
    d_win = nc.dram_tensor("ffw_in", [8, 128, 64], F32, kind="ExternalInput").ap()
    d_wout = nc.dram_tensor("ffw_out", [64, DM], F32, kind="ExternalInput").ap()
    d_ffs = nc.dram_tensor("ffs", [64, 50], F32, kind="ExternalInput").ap()
    d_fbr = nc.dram_tensor("fbr", [1, DM + 64], F32, kind="ExternalInput").ap()

    d_xlt = nc.dram_tensor("xlt", [8, 128, S], BF16, kind="Internal").ap()
    d_xltq = nc.dram_tensor("xltq", [8, 128, SL], BF16, kind="Internal").ap()
    d_osc = nc.dram_tensor("osc", [SL, DM], F32, kind="Internal").ap()
    d_x1sc = nc.dram_tensor("x1sc", [SL, DM], F32, kind="Internal").ap()
    d_y = nc.dram_tensor("y", [SL, DM], F32, kind="ExternalOutput").ap()

    A = nc.alloc_sbuf_tensor
    XW = A("XW", [128, 3 * DM], F32)
    XLN = A("XLN", [128, 2 * DM], BF16)
    ST = A("ST", [128, 24], F32)
    MV = A("MV", [128, 4], F32)
    T1 = A("T1", [128, 1], F32)
    RS = A("RS", [128, 2], F32)
    XLTS = A("XLTS", [128, 4 * 1024], BF16)
    XB = A("XB", [128, 2 * 8192], BF16)
    WST = A("WST", [128, 8 * KVC], F32)
    WKVP = A("WKVP", [128, 8 * KVC], BF16)
    WQP = A("WQP", [128, 8 * 128], BF16)
    BKV = A("BKV", [1, NP * KVC], BF16)
    BQ = A("BQ", [1, NP * 128], BF16)
    ONESR = A("ONESR", [1, 128], BF16)
    FFNW = A("FFNW", [128, 8 * 64], BF16)
    WOUT = A("WOUT", [64, DM], BF16)
    FFS = A("FFS", [64, 50], F32)
    BIN = A("BIN", [1, 64], BF16)
    BOUT = A("BOUT", [1, DM], BF16)
    CSB = A("CSB", [128, 2 * 1024], F32)
    CB = A("CB", [128, 8], F32)
    EYEB = A("EYEB", [128, 128], BF16)
    EYEF = A("EYEF", [128, 128], F32)
    KT = A("KT", [128, S], BF16)
    QT = A("QT", [128, SL], BF16)
    VAU = A("VAU", [128, NCH * 129], BF16)
    TRS = A("TRS", [128, 256], F32)
    KTOK = A("KTOK", [128, 8 * 128], BF16)
    PTA = A("PTA", [128, 2 * 1536], BF16)
    PTB = A("PTB", [128, 2 * 1536], BF16)
    OEA = A("OEA", [128, 512], F32)
    OEB = A("OEB", [128, 512], F32)
    RD = A("RD", [128, 4], F32)
    OTD = A("OTD", [128, 256], F32)
    OTD2 = A("OTD2", [128, 256], F32)
    OSCT = A("OSCT", [128, 2 * DM], F32)
    XQT = A("XQT", [128, 2 * DM], F32)
    X1 = A("X1", [128, 2 * DM], F32)
    TMP3 = A("TMP3", [128, DM], F32)
    X1N = A("X1N", [128, DM], BF16)
    X1NT = A("X1NT", [128, DM], BF16)
    SEV = A("SEV", [64, 128], F32)
    TM3 = A("TM3", [64, 1024], F32)
    G3 = A("G3", [64, 1024], F32)
    GB3 = A("GB3", [64, 1024], BF16)
    OUTS = A("OUTS", [128, DM], F32)
    YT = A("YT", [128, DM], F32)
    MVS3 = A("MVS3", [128, 16], F32)
    ST3 = A("ST3", [128, 12], F32)
    MV3 = A("MV3", [128, 2], F32)
    RSS3 = A("RSS3", [128, 16], F32)
    SALL = A("SALL", [64, 1024], F32)

    PX = nc.alloc_psum_tensor("PX", [128, 1536], F32)
    PY = nc.alloc_psum_tensor("PY", [128, 1536], F32)
    POA = nc.alloc_psum_tensor("POA", [128, 512], F32)
    POB = nc.alloc_psum_tensor("POB", [128, 512], F32)

    def pslot(s):
        if s < 3:
            return PX[:, s * 512:(s + 1) * 512]
        return PY[:, (s - 3) * 512:(s - 2) * 512]

    sem = {}
    for name in ("sXL sST sSTA sRS sXA sXT sXE sXWR sWL sWC "
                 "sXBL sCSL sPJ sRP sKTB sKTE sRC "
                 "sSA sSB sEA sEB sAVA sAVB sOEA sOEB sTRA sTRB sDVA sDVB sOSW "
                 "sOSR sXQL s3ST s3RS s3X1 s3A1 s3S2 s3N s3T sX1E s3S s3M s3O s3OE s3TT s3F sYW sDBG").split():
        sem[name] = nc.alloc_semaphore(name)

    P = Prog()

    wl = [0]
    wc = [0]

    def load(dst, src, eng="sp"):
        P.op(eng, lambda e, d=dst, s=src: e.dma_start(d, s), inc=("sWL", 16))
        wl[0] += 16

    load(CB[:], d_cb)
    load(EYEB[:], d_eyeb)
    load(EYEF[:], d_eyef)
    load(FFS[:], d_ffs)

    P.op("dve", lambda e: e.memset(ONESR[:], 1.0), inc=("sWC", 1)); wc[0] += 1

    P.op("sp", lambda e: e.dma_start(WST[0:1, 0:NP * KVC], d_bkv),
         waits=[("sWC", wc[0])], inc=("sWL", 16)); wl[0] += 16
    P.op("dve", lambda e: e.tensor_copy(BKV[:], WST[0:1, 0:NP * KVC]),
         waits=[("sWL", wl[0])], inc=("sWC", 1)); wc[0] += 1
    P.op("sp", lambda e: e.dma_start(WST[0:1, 0:NP * 128], d_bq),
         waits=[("sWC", wc[0])], inc=("sWL", 16)); wl[0] += 16
    P.op("dve", lambda e: e.tensor_copy(BQ[:], WST[0:1, 0:NP * 128]),
         waits=[("sWL", wl[0])], inc=("sWC", 1)); wc[0] += 1
    for dc in range(8):
        P.op("sp", lambda e, dc=dc: e.dma_start(WST[:, 0:64], d_win[dc]),
             waits=[("sWC", wc[0])], inc=("sWL", 16)); wl[0] += 16
        P.op("dve", lambda e, dc=dc: e.tensor_copy(FFNW[:, dc * 64:(dc + 1) * 64], WST[:, 0:64]),
             waits=[("sWL", wl[0])], inc=("sWC", 1)); wc[0] += 1
    P.op("sp", lambda e: e.dma_start(WST[0:64, 0:DM], d_wout),
         waits=[("sWC", wc[0])], inc=("sWL", 16)); wl[0] += 16
    P.op("dve", lambda e: e.tensor_copy(WOUT[:], WST[0:64, 0:DM]),
         waits=[("sWL", wl[0])], inc=("sWC", 1)); wc[0] += 1
    P.op("sp", lambda e: e.dma_start(WST[0:1, 0:DM + 64], d_fbr),
         waits=[("sWC", wc[0])], inc=("sWL", 16)); wl[0] += 16
    P.op("dve", lambda e: e.tensor_copy(BIN[:], WST[0:1, 0:64]),
         waits=[("sWL", wl[0])], inc=("sWC", 1)); wc[0] += 1
    P.op("dve", lambda e: e.tensor_copy(BOUT[:], WST[0:1, 64:64 + DM]),
         inc=("sWC", 1)); wc[0] += 1
    W_ALL = wl[0]
    WC_ALL = wc[0]

    def ph1():
        nxl = [0]
        NT = NB + NQB

        def src_of(t):
            isq = t >= NB
            tt = t - NB if isq else t
            return (d_xq if isq else d_x)[tt * 128:(tt + 1) * 128, :], isq, tt

        def emit_stats(t):
            src, isq, tt = src_of(t)
            b3 = t % 3
            buf = t % 2
            w = [("sSTA", t - 2)] if t >= 3 else []
            if t == 0:
                w.append(("sWL", W_ALL))
            P.op("sp", lambda e, s_=src, b=b3: e.dma_start(XW[:, b * DM:(b + 1) * DM], s_),
                 waits=w, inc=("sXL", 16)); nxl[0] += 16
            w2 = [("sXL", nxl[0])]
            if t >= 2:
                w2.append(("sRS", t - 1))
            P.op("dve", lambda e, b=buf, b3=b3: e.bn_stats(ST[:, b * 6:(b + 1) * 6], XW[:, b3 * DM:b3 * DM + 512]),
                 waits=w2)
            P.op("dve", lambda e, b=buf, b3=b3: e.bn_stats(ST[:, 12 + b * 6:12 + (b + 1) * 6],
                                                    XW[:, b3 * DM + 512:b3 * DM + 1024]))

        def emit_aggr(t):
            buf = t % 2
            ap_in = ST[:].rearrange("p (h x) -> p h x", h=2)[:, :, 6 * buf:6 * buf + 6]
            if t == 0:
                P.op("dve", lambda e, a=ap_in: (e.bn_aggr(MV[:, 0:2], a), e.drain())[0],
                     inc=("sST", 1), flush=True)
            else:
                P.op("dve", lambda e, b=buf, a=ap_in: e.bn_aggr(MV[:, 2 * b:2 * b + 2], a))

        def emit_rstd(t):
            buf = t % 2
            wl_ = [("sSTA", t)] if t >= 1 else [("sST", 1)]
            P.op("act", lambda e, b=buf: e.activation(T1[:], MV[:, 2 * b + 1:2 * b + 2], AF.Ln,
                                                      bias=CB[:, C_EPS:C_EPS + 1], scale=CB[:, C_ONE:C_ONE + 1]),
                 waits=wl_)
            P.op("act", lambda e, b=buf: e.activation(RS[:, b:b + 1], T1[:], AF.Exp,
                                                      bias=CB[:, C_ZERO:C_ZERO + 1],
                                                      scale=CB[:, C_NHALF:C_NHALF + 1]),
                 inc=("sRS", 1))

        def emit_apply(t):
            buf = t % 2
            b3 = t % 3
            w = [("sRS", t + 1)]
            if t >= 2:
                w.append(("sXT", t - 1))
            P.op("dve", lambda e, b=buf, b3=b3: e.tensor_scalar(
                XLN[:, b * DM:(b + 1) * DM], XW[:, b3 * DM:(b3 + 1) * DM],
                MV[:, 2 * b:2 * b + 1], RS[:, b:b + 1], ALU.subtract, ALU.mult),
                waits=w, inc=("sSTA", 1))

        def emit_tr_evac(t):
            _, isq, tt = src_of(t)
            buf = t % 2
            dst_dram = d_xltq if isq else d_xlt
            slab = tt // 8
            PXb = PX[:, 0:512].bitcast(BF16)

            def tr(e, b=buf):
                last = None
                for j in range(8):
                    last = e.transpose(PXb[:, j * 128:(j + 1) * 128],
                                       XLN[:, b * DM + j * 128:b * DM + (j + 1) * 128], EYEB[:])
                return last
            P.op("pe", tr, waits=[("sSTA", t + 1), ("sXE", t)], inc=("sXT", 1))
            w = [("sXT", t + 1)]
            if t >= 4:
                w.append(("sXWR", 16 * (t // 4)))

            def ev(e, tt=tt):
                src_ap = PXb[:, 0:1024].rearrange("p (j u) -> p j u", j=8)
                dst_ap = XLTS[:].rearrange("p (j u) -> p j u", j=8)[:, :, (tt % 4) * 128:(tt % 4 + 1) * 128]
                return e.tensor_copy(dst_ap, src_ap)
            P.op("dve", ev, waits=w, inc=("sXE", 1), flush=True)
            if t % 4 == 3:
                sl4 = tt // 4
                def wr(e, dd=dst_dram, sl=sl4):
                    dst = dd[:, :, sl * 512:(sl + 1) * 512].rearrange("j p t -> p j t")
                    src = XLTS[:].rearrange("p (j u) -> p j u", j=8)
                    return e.dma_start(dst, src)
                P.op("pool", wr, waits=[("sXE", t + 1)], inc=("sXWR", 16))

        for t in range(NT + 2):
            if t < NT:
                emit_stats(t)
            if 1 <= t <= NT:
                emit_aggr(t - 1)
                emit_rstd(t - 1)
            if 2 <= t <= NT + 1:
                emit_apply(t - 2)
                emit_tr_evac(t - 2)
    ph1()
    XWR_ALL = 16 * ((NB + NQB) // 4)

    slab_hist = {0: [], 1: []}
    gslab = [0]

    def emit_a1(i):
        buf = i % 2
        qwi = i // 4
        qq7 = NP * NQW - NQW + qwi
        w = [("sOSW", 16 * 8 * (qq7 + 1))]
        if i >= 2:
            w.append(("s3X1", 16 * (i - 1)))
        P.op("sp", lambda e, i=i, b=buf: e.dma_start(OSCT[:, b * DM:(b + 1) * DM],
                                                     d_osc[i * 128:(i + 1) * 128, :]),
             waits=w, inc=("sOSR", 16))
        wx = [("s3X1", 16 * (i - 1))] if i >= 2 else []
        P.op("sp", lambda e, i=i, b=buf: e.dma_start(XQT[:, b * DM:(b + 1) * DM],
                                                     d_xq[i * 128:(i + 1) * 128, :]),
             waits=wx, inc=("sXQL", 16))
        P.op("dve", lambda e, b=buf: e.bn_stats(ST3[:, 0:6], OSCT[:, b * DM:b * DM + 512]),
             waits=[("sOSR", 16 * (i + 1)), ("s3RS", 2 * i)])
        P.op("dve", lambda e, b=buf: e.bn_stats(ST3[:, 6:12], OSCT[:, b * DM + 512:b * DM + 1024]))
        P.op("dve", lambda e: (e.drain(), e.bn_aggr(MV3[:], ST3[:]))[1], inc=("s3ST", 1), flush=True)
        P.op("act", lambda e: e.activation(T1[:], MV3[:, 1:2], AF.Ln,
                                           bias=CB[:, C_EPS:C_EPS + 1], scale=CB[:, C_ONE:C_ONE + 1]),
             waits=[("s3ST", 2 * i + 1)])
        P.op("act", lambda e: e.activation(RS[:, 0:1], T1[:], AF.Exp,
                                           bias=CB[:, C_ZERO:C_ZERO + 1], scale=CB[:, C_NHALF:C_NHALF + 1]),
             inc=("s3RS", 1), flush=True)
        P.op("dve", lambda e, b=buf: e.tensor_scalar(TMP3[:], OSCT[:, b * DM:(b + 1) * DM],
                                                     MV3[:, 0:1], RS[:, 0:1], ALU.subtract, ALU.mult),
             waits=[("s3RS", 2 * i + 1), ("sXQL", 16 * (i + 1))])
        P.op("dve", lambda e, b=buf: e.tensor_add(X1[:, b * DM:(b + 1) * DM], TMP3[:],
                                                  XQT[:, b * DM:(b + 1) * DM]),
             inc=("s3A1", 1), flush=True)
        P.op("dve", lambda e, b=buf: e.bn_stats(ST3[:, 0:6], X1[:, b * DM:b * DM + 512]))
        P.op("dve", lambda e, b=buf: e.bn_stats(ST3[:, 6:12], X1[:, b * DM + 512:b * DM + 1024]))
        P.op("dve", lambda e: (e.drain(), e.bn_aggr(MV3[:], ST3[:]))[1], inc=("s3ST", 1), flush=True)
        P.op("act", lambda e: e.activation(T1[:], MV3[:, 1:2], AF.Ln,
                                           bias=CB[:, C_EPS:C_EPS + 1], scale=CB[:, C_ONE:C_ONE + 1]),
             waits=[("s3ST", 2 * i + 2)])
        P.op("act", lambda e, i=i: e.activation(RSS3[:, i:i + 1], T1[:], AF.Exp,
                                                bias=CB[:, C_ZERO:C_ZERO + 1], scale=CB[:, C_NHALF:C_NHALF + 1]),
             inc=("s3RS", 1), flush=True)
        P.op("dve", lambda e, i=i: e.tensor_copy(MVS3[:, i:i + 1], MV3[:, 0:1]),
             waits=[("s3RS", 2 * i + 2)])
        P.op("pool", lambda e, i=i, b=buf: e.dma_start(d_x1sc[i * 128:(i + 1) * 128, :],
                                                       X1[:, b * DM:(b + 1) * DM]),
             waits=[("s3A1", i + 1)], inc=("s3X1", 16))


    def emit_pair(p):
        rb0 = p * RBT
        src_kv = d_wkv[:, :, p * KVC:(p + 1) * KVC].rearrange("j p c -> p j c")
        P.op("sp", lambda e, s_=src_kv: e.dma_start(
            WST[:].rearrange("p (j c) -> p j c", j=8), s_),
             waits=[("sWC", wc[0])], inc=("sWL", 16)); wl[0] += 16
        P.op("dve", lambda e, p=p: e.tensor_copy(WKVP[:], WST[:]),
             waits=[("sWL", wl[0]), ("sPJ", p * RBT)], inc=("sWC", 1), flush=True); wc[0] += 1
        src_q = d_wq[:, :, p * 128:(p + 1) * 128].rearrange("j p c -> p j c")
        P.op("sp", lambda e, s_=src_q: e.dma_start(
            WST[:, 0:1024].rearrange("p (j c) -> p j c", j=8), s_),
             waits=[("sWC", wc[0])], inc=("sWL", 16)); wl[0] += 16
        P.op("dve", lambda e, p=p: e.tensor_copy(WQP[:], WST[:, 0:1024]),
             waits=[("sWL", wl[0]), ("sPJ", p * RBT)], inc=("sWC", 1), flush=True); wc[0] += 1
        wc_pair = wc[0]
        for si in range(NSLAB):
            g = gslab[0]; gslab[0] += 1
            buf = g % 2
            isq = si >= 8
            w = []
            if slab_hist[buf]:
                w.append(("sPJ", slab_hist[buf][-1]))
            if g < 2:
                w.append(("sXWR", XWR_ALL))
            if isq:
                qsl = si - 8
                src = d_xltq[:, :, qsl * 1024:(qsl + 1) * 1024].rearrange("j p t -> p j t")
            else:
                src = d_xlt[:, :, si * 1024:(si + 1) * 1024].rearrange("j p t -> p j t")
            dst = XB[:, buf * 8192:(buf + 1) * 8192].rearrange("p (j t) -> p j t", j=8)
            P.op("sp", lambda e, d=dst, s=src: e.dma_start(d, s), waits=w, inc=("sXBL", 16))
            slab_hist[buf].append(rb0 + min((si + 1) * 8, RBT))
            if isq:
                qsl = si - 8
                c_src = d_csq[:, qsl * 512:(qsl + 1) * 512]
                s_src = d_csq[:, NQB * 64 + qsl * 512: NQB * 64 + (qsl + 1) * 512]
            else:
                c_src = d_csk[:, si * 512:(si + 1) * 512]
                s_src = d_csk[:, NB * 64 + si * 512: NB * 64 + (si + 1) * 512]
            wcs = [("sRP", slab_hist[buf][-2])] if len(slab_hist[buf]) >= 2 else []
            P.op("sp", lambda e, d=CSB, s_=c_src, b=buf: e.dma_start(d[:, b * 1024:b * 1024 + 512], s_),
                 waits=wcs, inc=("sCSL", 16))
            P.op("sp", lambda e, d=CSB, s_=s_src, b=buf: e.dma_start(d[:, b * 1024 + 512:b * 1024 + 1024], s_),
                 waits=[], inc=("sCSL", 16))

        for ib in range(RBT):
            rb = rb0 + ib
            isq = ib >= NB
            si = ib // 8
            buf = si % 2
            bin_slab = ib % 8
            w = [("sXBL", 16 * (p * NSLAB + si + 1))]
            if rb >= 6:
                w.append(("sRP", rb - 5))
            if ib == 0 and p > 0:
                w += [("sDVA", p * NQW), ("sDVB", p * NQW)]
            if ib == 0:
                w.append(("sWC", wc_pair))
            slot = pslot(rb % 6)
            ncols = 128 if isq else KVC
            def proj(e, p=p, buf=buf, bs=bin_slab, isq=isq, slot=slot, ncols=ncols):
                last = None
                for dc in range(8):
                    lhsT = XB[:, buf * 8192 + dc * 1024 + bs * 128: buf * 8192 + dc * 1024 + (bs + 1) * 128]
                    if isq:
                        rhs = WQP[:, dc * 128:(dc + 1) * 128]
                    else:
                        rhs = WKVP[:, dc * KVC:(dc + 1) * KVC]
                    last = e.matmul(slot[:, 0:ncols], lhsT, rhs, start=(dc == 0), stop=False)
                bias = BQ[:, p * 128:(p + 1) * 128] if isq else BKV[:, p * KVC:(p + 1) * KVC]
                last = e.matmul(slot[:, 0:ncols], ONESR[:], bias, start=False, stop=True)
                return last
            P.op("pe", proj, waits=w, inc=("sPJ", 1))

            w = [("sPJ", rb + 1)]
            if rb >= 8:
                w.append(("sKTB", rb // 8))
            blk = ib if not isq else (ib - NB)
            csbuf = si % 2
            co = csbuf * 1024 + (ib % 8) * 64
            w.append(("sCSL", 32 * (p * NSLAB + si + 1)))

            def rope2(e, slot=slot, co=co, kslot=rb % 8, isq=isq, blk=blk):
                xr = slot[:, 0:128:2]
                xi = slot[:, 1:128:2]
                c_ap = CSB[:, co:co + 64]
                s_ap = CSB[:, co + 512:co + 512 + 64]
                e.tensor_mul(TRS[:, 0:64], xr, c_ap)
                e.tensor_mul(TRS[:, 64:128], xi, s_ap)
                e.tensor_mul(TRS[:, 128:192], xr, s_ap)
                e.tensor_mul(TRS[:, 192:256], xi, c_ap)
                kt = KTOK[:, kslot * 128:(kslot + 1) * 128]
                e.tensor_sub(kt[:, 0:128:2], TRS[:, 0:64], TRS[:, 64:128])
                last = e.tensor_add(kt[:, 1:128:2], TRS[:, 128:192], TRS[:, 192:256])
                if not isq:
                    last = e.tensor_copy(VAU[:, blk * 129:(blk + 1) * 129], slot[:, 128:KVC])
                return last
            P.op("dve", rope2, waits=w, inc=("sRP", 1), flush=(ib % 8 == 7))

            if ib % 8 == 7:
                bidx = rb // 8 + 1
                POAb = POA[:].bitcast(BF16)
                def trb(e):
                    last = None
                    for j in range(8):
                        last = e.transpose(POAb[:, j * 128:(j + 1) * 128],
                                           KTOK[:, j * 128:(j + 1) * 128], EYEB[:])
                    return last
                P.op("pe", trb, waits=[("sRP", rb + 1), ("sKTE", rb // 8)], inc=("sKTB", 1))
                dstT = QT if isq else KT
                off = (blk - 7) * 128
                P.op("dve", lambda e, d=dstT, off=off: e.tensor_copy(
                    d[:, off:off + 1024], POA[:].bitcast(BF16)[:, 0:1024]),
                    waits=[("sKTB", rb // 8 + 1)], inc=("sKTE", 1), flush=True)
        P.op("dve", lambda e: e.engine_nop(), waits=[("sKTE", (rb0 + RBT) // 8)], inc=("sRC", 1))

        for wq_i in range(NQW):
            base_u = (p * NQW + wq_i) * NGR
            qoff = wq_i * QW
            for g in range(NGR):
                u = base_u + g
                glen = GRP[g]
                cbase = sum(GRP[:g])
                w = []
                if u > 0:
                    w.append(("sEA", u))
                if g == 0 and wq_i == 0:
                    w.append(("sRC", p + 1))
                def sa(e, glen=glen, cbase=cbase, qoff=qoff):
                    last = None
                    for j in range(glen):
                        c = cbase + j
                        last = e.matmul(PX[:, j * 512:(j + 1) * 512],
                                        KT[0:64, c * 128:(c + 1) * 128],
                                        QT[0:64, qoff:qoff + QW], start=True, stop=True)
                    return last
                P.op("pe", sa, waits=w, inc=("sSA", 1))
                w = [("sEB", u)] if u > 0 else []
                def sb(e, glen=glen, cbase=cbase, qoff=qoff):
                    last = None
                    for j in range(glen):
                        c = cbase + j
                        last = e.matmul(PY[:, j * 512:(j + 1) * 512],
                                        KT[64:128, c * 128:(c + 1) * 128],
                                        QT[64:128, qoff:qoff + QW],
                                        start=True, stop=True, tile_position=(64, 0))
                    return last
                P.op("pe", sb, waits=w, inc=("sSB", 1))
                if g > 0:
                    _emit_av(P, p, wq_i, g - 1, base_u + g - 1, PTA, VAU, POA, 0, "sEA", "sAVA", "sDVA")
                    _emit_av(P, p, wq_i, g - 1, base_u + g - 1, PTB, VAU, POB, 64, "sEB", "sAVB", "sDVB")
            _emit_av(P, p, wq_i, NGR - 1, base_u + NGR - 1, PTA, VAU, POA, 0, "sEA", "sAVA", "sDVA")
            _emit_av(P, p, wq_i, NGR - 1, base_u + NGR - 1, PTB, VAU, POB, 64, "sEB", "sAVB", "sDVB")

            for g in range(NGR):
                u = base_u + g
                glen = GRP[g]
                fd = glen * 512
                buf = u % 2
                w = [("sSA", u + 1)]
                if u >= 2:
                    w.append(("sAVA", u - 1))
                P.op("act", lambda e, fd=fd, buf=buf: e.activation(
                    PTA[:, buf * 1536:buf * 1536 + fd], PX[:, 0:fd], AF.Exp,
                    bias=CB[:, C_SHIFT:C_SHIFT + 1], scale=CB[:, C_SCL:C_SCL + 1]),
                    waits=w, inc=("sEA", 1))
                w = [("sSB", u + 1)]
                if u >= 2:
                    w.append(("sAVB", u - 1))
                P.op("act", lambda e, fd=fd, buf=buf: e.activation(
                    PTB[:, buf * 1536:buf * 1536 + fd], PY[:, 0:fd], AF.Exp,
                    bias=CB[:, C_SHIFT:C_SHIFT + 1], scale=CB[:, C_SCL:C_SCL + 1]),
                    waits=w, inc=("sEB", 1))

            qq = p * NQW + wq_i
            P.op("dve", lambda e: e.tensor_copy(OEA[0:65, :], POA[0:65, :]),
                 waits=[("sAVA", (qq + 1) * NGR)], inc=("sOEA", 1), flush=True)
            P.op("dve", lambda e: e.tensor_copy(OEB[0:65, :], POB[0:65, :]),
                 waits=[("sAVB", (qq + 1) * NGR)], inc=("sOEB", 1), flush=True)
            def tra(e):
                last = None
                for j in range(4):
                    last = e.transpose(POA[:, j * 128:(j + 1) * 128], OEA[:, j * 128:(j + 1) * 128], EYEF[:])
                return last
            P.op("pe", tra, waits=[("sOEA", qq + 1)], inc=("sTRA", 1))
            def trb_(e):
                last = None
                for j in range(4):
                    last = e.transpose(POB[:, j * 128:(j + 1) * 128], OEB[:, j * 128:(j + 1) * 128], EYEF[:])
                return last
            P.op("pe", trb_, waits=[("sOEB", qq + 1)], inc=("sTRB", 1))
            w = [("sTRA", qq + 1)]
            if qq > 0:
                w.append(("sOSW", 16 * 8 * qq))
            def dva(e):
                for j in range(4):
                    e.reciprocal(RD[:, j:j + 1], POA[:, j * 128 + 64:j * 128 + 65])
                e.drain()
                last = None
                for j in range(4):
                    last = e.tensor_scalar(OTD[:, j * 64:(j + 1) * 64], POA[:, j * 128:j * 128 + 64],
                                           RD[:, j:j + 1], None, ALU.mult)
                return last
            P.op("dve", dva, waits=w, inc=("sDVA", 1), flush=True)
            w = [("sTRB", qq + 1)]
            if qq > 0:
                w.append(("sOSW", 16 * 8 * qq))
            def dvb(e):
                for j in range(4):
                    e.reciprocal(RD[:, j:j + 1], POB[:, j * 128:j * 128 + 1])
                e.drain()
                last = None
                for j in range(4):
                    last = e.tensor_scalar(OTD2[:, j * 64:(j + 1) * 64], POB[:, j * 128 + 1:j * 128 + 65],
                                           RD[:, j:j + 1], None, ALU.mult)
                return last
            P.op("dve", dvb, waits=w, inc=("sDVB", 1), flush=True)
            for j in range(4):
                dst = d_osc[qoff + j * 128: qoff + (j + 1) * 128, (2 * p) * 64:(2 * p + 1) * 64]
                P.op("pool", lambda e, d=dst, j=j: e.dma_start(d, OTD[:, j * 64:(j + 1) * 64]),
                     waits=[("sDVA", qq + 1)], inc=("sOSW", 16))
            for j in range(4):
                dst = d_osc[qoff + j * 128: qoff + (j + 1) * 128, (2 * p + 1) * 64:(2 * p + 2) * 64]
                P.op("pool", lambda e, d=dst, j=j: e.dma_start(d, OTD2[:, j * 64:(j + 1) * 64]),
                     waits=[("sDVB", qq + 1)], inc=("sOSW", 16))
            if p == NP - 1:
                for ti in range(wq_i * 4, wq_i * 4 + 4):
                    emit_a1(ti)

    if PHASES == "all":
        for p in range(NP):
            emit_pair(p)

    for half in (range(2) if PHASES == "all" else []):
        for i in range(half * 8, half * 8 + 8):
            buf = i % 2
            w = [("s3X1", 16 * 16)]
            if i >= 2:
                w.append(("s3N", i - 1))
            P.op("sp", lambda e, i=i, b=buf: e.dma_start(X1[:, b * DM:(b + 1) * DM],
                                                         d_x1sc[i * 128:(i + 1) * 128, :]),
                 waits=w, inc=("sOSR", 16))
            wap = [("sOSR", 16 * (17 + i))]
            if i >= 1:
                wap.append(("s3T", i))
            P.op("dve", lambda e, i=i, b=buf: e.tensor_scalar(X1N[:], X1[:, b * DM:(b + 1) * DM],
                                                              MVS3[:, i:i + 1], RSS3[:, i:i + 1],
                                                              ALU.subtract, ALU.mult),
                 waits=wap, inc=None, flush=False)
            PXb = PX[:, 0:512].bitcast(BF16)
            def t3(e):
                last = None
                for j in range(8):
                    last = e.transpose(PXb[:, j * 128:(j + 1) * 128], X1N[:, j * 128:(j + 1) * 128], EYEB[:])
                return last
            wpe = [("sX1E", i)] if i >= 1 else [("sDVA", NP * NQW), ("sDVB", NP * NQW)]
            P.op("dve", lambda e: e.drain(), inc=("s3N", 1))
            P.op("pe", t3, waits=[("s3N", i + 1)] + wpe, inc=("s3T", 1))
            P.op("dve", lambda e: e.tensor_copy(X1NT[:], PXb[:, 0:1024]),
                 waits=[("s3T", i + 1)], inc=("sX1E", 1), flush=True)
            def smm(e):
                for dc in range(8):
                    e.matmul(PY[0:64, 0:128], FFNW[:, dc * 64:(dc + 1) * 64],
                             X1NT[:, dc * 128:(dc + 1) * 128], start=(dc == 0), stop=False)
                return e.matmul(PY[0:64, 0:128], BIN[:], ONESR[:, 0:128], start=False, stop=True)
            P.op("pe", smm, waits=[("sX1E", i + 1), ("s3S2", i)], inc=("s3S", 1))
            P.op("dve", lambda e, i=i: e.tensor_copy(SALL[:, (i % 8) * 128:(i % 8 + 1) * 128],
                                                     PY[0:64, 0:128]),
                 waits=[("s3S", i + 1)], inc=("s3S2", 1), flush=True)
        def mid(e, half=half):
            for h in range(16):
                e.tensor_scalar(TM3[:], SALL[:, 0:1024], FFS[0:64, h:h + 1], FFS[0:64, 16 + h:17 + h],
                                ALU.mult, ALU.add)
                e.tensor_scalar(TM3[:], TM3[:], CB[0:64, C_ZERO:C_ZERO + 1], None, ALU.max)
                if h == 0:
                    e.tensor_scalar(G3[:], TM3[:], FFS[0:64, 32:33], FFS[0:64, 48:49],
                                    ALU.mult, ALU.add)
                else:
                    e.scalar_tensor_tensor(G3[:], TM3[:], FFS[0:64, 32 + h:33 + h], G3[:],
                                           ALU.mult, ALU.add)
                e.drain()
            return e.tensor_copy(GB3[:], G3[:])
        wmid = [("s3S2", half * 8 + 8)]
        if half == 1:
            wmid.append(("s3O", 8))
        P.op("dve", mid, waits=wmid, inc=("s3M", 1), flush=True)
        for i in range(half * 8, half * 8 + 8):
            buf = i % 2
            w = [("s3M", half + 1)]
            if i >= 2:
                w.append(("s3F", i - 1))
            P.op("sp", lambda e, i=i, b=buf: e.dma_start(XQT[:, b * DM:(b + 1) * DM],
                                                         d_x1sc[i * 128:(i + 1) * 128, :]),
                 waits=w, inc=("sXQL", 16))
            def omm(e, i=i):
                last = None
                for dc in range(8):
                    ps = PX[:, 512 + dc * 128:512 + (dc + 1) * 128]
                    e.matmul(ps, WOUT[:, dc * 128:(dc + 1) * 128], GB3[:, (i % 8) * 128:(i % 8 + 1) * 128],
                             start=True, stop=False)
                    last = e.matmul(ps, BOUT[:, dc * 128:(dc + 1) * 128], ONESR[:, 0:128], start=False, stop=True)
                return last
            P.op("pe", omm, waits=[("s3M", half + 1), ("s3OE", i)], inc=("s3O", 1))
            P.op("dve", lambda e: e.tensor_copy(OUTS[:], PX[:, 512:1536]),
                 waits=[("s3O", i + 1)], inc=("s3OE", 1), flush=True)
            def t3b(e):
                last = None
                for dc in range(8):
                    last = e.transpose(PY[:, 512 + dc * 128:512 + (dc + 1) * 128],
                                       OUTS[:, dc * 128:(dc + 1) * 128], EYEF[:])
                return last
            P.op("pe", t3b, waits=[("s3OE", i + 1), ("s3F", i)], inc=("s3TT", 1))
            P.op("dve", lambda e: e.bn_stats(ST3[:, 0:6], PY[:, 512:1024]), waits=[("s3TT", i + 1)])
            P.op("dve", lambda e: e.bn_stats(ST3[:, 6:12], PY[:, 1024:1536]))
            P.op("dve", lambda e: (e.drain(), e.bn_aggr(MV3[:], ST3[:]))[1], inc=("s3ST", 1), flush=True)
            P.op("act", lambda e: e.activation(T1[:], MV3[:, 1:2], AF.Ln,
                                               bias=CB[:, C_EPS:C_EPS + 1], scale=CB[:, C_ONE:C_ONE + 1]),
                 waits=[("s3ST", 32 + i + 1)])
            P.op("act", lambda e: e.activation(RS[:, 0:1], T1[:], AF.Exp,
                                               bias=CB[:, C_ZERO:C_ZERO + 1], scale=CB[:, C_NHALF:C_NHALF + 1]),
                 inc=("s3RS", 1), flush=True)
            P.op("dve", lambda e: e.tensor_scalar(TMP3[:], PY[:, 512:1536], MV3[:, 0:1], RS[:, 0:1],
                                                  ALU.subtract, ALU.mult),
                 waits=[("s3RS", 32 + i + 1)])
            wyt = [("sYW", 16 * i)] if i >= 1 else []
            P.op("dve", lambda e, b=buf: e.tensor_add(YT[:], TMP3[:], XQT[:, b * DM:(b + 1) * DM]),
                 waits=wyt + [("sXQL", 16 * (17 + i))], inc=("s3F", 1), flush=True)
            P.op("pool", lambda e, i=i: e.dma_start(d_y[i * 128:(i + 1) * 128, :], YT[:]),
                 waits=[("s3F", i + 1)], inc=("sYW", 16))

    FINAL_WAIT = ("sYW", 16 * 16) if PHASES == "all" else ("sXWR", XWR_ALL)
    P.op("sp", lambda e: e.nop(), waits=[FINAL_WAIT])
    if DEBUG_DUMPS:
        d_dosc = nc.dram_tensor("dbg_osc", [SL, DM], F32, kind="ExternalOutput").ap()
        d_dxlt = nc.dram_tensor("dbg_xlt", [8, 128, 1024], BF16, kind="ExternalOutput").ap()
        d_dxlq = nc.dram_tensor("dbg_xltq", [8, 128, 1024], BF16, kind="ExternalOutput").ap()
        d_dkt = nc.dram_tensor("dbg_kt", [128, S], BF16, kind="ExternalOutput").ap()
        d_dqt = nc.dram_tensor("dbg_qt", [128, SL], BF16, kind="ExternalOutput").ap()
        d_dva = nc.dram_tensor("dbg_vau", [128, 10 * 129], BF16, kind="ExternalOutput").ap()
        P.op("sp", lambda e: e.dma_start(d_dosc, d_osc[:]), waits=[FINAL_WAIT], inc=("sDBG", 16))
        P.op("sp", lambda e: e.dma_start(d_dxlt, d_xlt[:, :, 0:1024]), inc=("sDBG", 16))
        P.op("sp", lambda e: e.dma_start(d_dxlq, d_xltq[:, :, 0:1024]), inc=("sDBG", 16))
        P.op("sp", lambda e: e.dma_start(d_dkt, KT[:]), inc=("sDBG", 16))
        P.op("sp", lambda e: e.dma_start(d_dqt, QT[:]), inc=("sDBG", 16))
        P.op("sp", lambda e: e.dma_start(d_dva, VAU[:, 0:10 * 129]), inc=("sDBG", 16))
        P.op("sp", lambda e: e.nop(), waits=[("sDBG", 96)])

    build_nc.last_prog = P
    with nc.Block() as block:
        def body_for(name):
            def body(eng):
                for waits, fn, inc, flush in P.items[name]:
                    for (s, v) in waits:
                        if v > 0:
                            eng.wait_ge(sem[s], v)
                    instr = fn(eng)
                    if inc is not None:
                        if flush:
                            eng.drain().then_inc(sem[inc[0]], inc[1])
                        else:
                            instr.then_inc(sem[inc[0]], inc[1])
            return body
        block.sync(body_for("sp"))
        block.tensor(body_for("pe"))
        block.scalar(body_for("act"))
        block.vector(body_for("dve"))
        block.gpsimd(body_for("pool"))

    return nc


def _emit_av(P, p, wq_i, g, u, PT, VAU, PO, voff, sE, sAV, sDV):
    glen = GRP[g]
    cbase = sum(GRP[:g])
    buf = u % 2
    qq = p * NQW + wq_i
    waits = [(sE, u + 1)]
    if g == 0 and qq > 0:
        waits.append((sDV, qq))
    def av(e, glen=glen, cbase=cbase, buf=buf, g=g, voff=voff):
        last = None
        for j in range(glen):
            c = cbase + j
            last = e.matmul(PO[0:65, 0:QW],
                            VAU[:, c * 129 + voff: c * 129 + voff + 65],
                            PT[:, buf * 1536 + j * 512: buf * 1536 + (j + 1) * 512],
                            start=(g == 0 and j == 0), stop=(g == NGR - 1 and j == glen - 1))
        return last
    P.op("pe", av, waits=waits, inc=(sAV, 1))


_NC_CACHE = {}


def _get_nc():
    if "nc" not in _NC_CACHE:
        _NC_CACHE["nc"] = build_nc()
    return _NC_CACHE["nc"]


def kernel(L0_attn_x, L0_attn_wq_w, L0_attn_wq_b, L0_attn_wk_w, L0_attn_wk_b,
           L0_attn_wv_w, L0_attn_wv_b, L0_ffn_Win, L0_ffn_bin_, L0_ffn_W1,
           L0_ffn_b1, L0_ffn_W2, L0_ffn_b2, L0_ffn_Wout, L0_ffn_bout):
    x = np.asarray(L0_attn_x, dtype=np.float32)
    wq = np.asarray(L0_attn_wq_w, np.float32)
    wk = np.asarray(L0_attn_wk_w, np.float32)
    wv = np.asarray(L0_attn_wv_w, np.float32)
    bq = np.asarray(L0_attn_wq_b, np.float32)
    bk = np.asarray(L0_attn_wk_b, np.float32)
    bv = np.asarray(L0_attn_wv_b, np.float32)

    wqT = np.ascontiguousarray(wq.T)
    wkT = np.ascontiguousarray(wk.T)
    wvT = np.ascontiguousarray(wv.T)

    wkv_h = np.zeros((8, 128, NP * KVC), np.float32)
    bkv_h = np.zeros((1, NP * KVC), np.float32)
    wq_h = np.zeros((8, 128, NP * 128), np.float32)
    bq_h = np.zeros((1, NP * 128), np.float32)
    for p in range(NP):
        hA, hB = 2 * p, 2 * p + 1
        colsK = slice(p * KVC, p * KVC + 128)
        colsVA = slice(p * KVC + 128, p * KVC + 192)
        colsVB = slice(p * KVC + 193, p * KVC + 257)
        for dc in range(8):
            din = slice(dc * 128, (dc + 1) * 128)
            wkv_h[dc, :, colsK] = wkT[din, hA * 64:hB * 64 + 64]
            wkv_h[dc, :, colsVA] = wvT[din, hA * 64:(hA + 1) * 64]
            wkv_h[dc, :, colsVB] = wvT[din, hB * 64:(hB + 1) * 64]
            wq_h[dc, :, p * 128:(p + 1) * 128] = wqT[din, hA * 64:hB * 64 + 64]
        bkv_h[0, colsK] = bk[hA * 64:hB * 64 + 64]
        bkv_h[0, colsVA] = bv[hA * 64:(hA + 1) * 64]
        bkv_h[0, p * KVC + 192] = 1.0
        bkv_h[0, colsVB] = bv[hB * 64:(hB + 1) * 64]
        bq_h[0, p * 128:(p + 1) * 128] = bq[hA * 64:hB * 64 + 64]

    theta = (10000.0 ** (-(np.arange(32, dtype=np.float64) / 32.0))).astype(np.float64)
    pos = np.arange(S, dtype=np.float64)
    ang = np.outer(pos, theta)
    cosf = np.cos(ang).astype(np.float32)
    sinf = np.sin(ang).astype(np.float32)

    def mk_cs(tokens):
        nb = len(tokens) // 128
        c = np.zeros((128, nb * 64), np.float32)
        s = np.zeros((128, nb * 64), np.float32)
        for b in range(nb):
            tt = tokens[b * 128:(b + 1) * 128]
            cc = cosf[tt]
            ss = sinf[tt]
            c[:, b * 64:b * 64 + 32] = cc
            c[:, b * 64 + 32:b * 64 + 64] = cc
            s[:, b * 64:b * 64 + 32] = ss
            s[:, b * 64 + 32:b * 64 + 64] = ss
        return np.concatenate([c, s], axis=1)

    csk_h = mk_cs(np.arange(S))

    cb_h = np.zeros((128, 8), np.float32)
    cb_h[:, C_SHIFT] = -SHIFT
    cb_h[:, C_SCL] = 1.0 / np.sqrt(HD)
    cb_h[:, C_ONE] = 1.0
    cb_h[:, C_ZERO] = 0.0
    cb_h[:, C_NHALF] = -0.5
    cb_h[:, C_EPS] = EPS

    eyeb_h = np.eye(128).astype(BF)
    eyef_h = np.eye(128).astype(np.float32)

    win_h = np.asarray(L0_ffn_Win, np.float32).reshape(8, 128, 64)
    wout_h = np.asarray(L0_ffn_Wout, np.float32)
    ffs_h = np.zeros((64, 50), np.float32)
    ffs_h[:, 0:16] = np.asarray(L0_ffn_W1, np.float32)
    ffs_h[:, 16:32] = np.asarray(L0_ffn_b1, np.float32)
    ffs_h[:, 32:48] = np.asarray(L0_ffn_W2, np.float32)
    ffs_h[:, 48] = np.asarray(L0_ffn_b2, np.float32)
    fbr_h = np.zeros((1, DM + 64), np.float32)
    fbr_h[0, 0:64] = np.asarray(L0_ffn_bin_, np.float32)
    fbr_h[0, 64:] = np.asarray(L0_ffn_bout, np.float32)

    shared = {
        "wkv": wkv_h, "wq": wq_h, "bkv": bkv_h, "bq": bq_h,
        "csk": csk_h, "cbv": cb_h, "eyeb": eyeb_h, "eyef": eyef_h,
        "ffw_in": win_h, "ffw_out": wout_h, "ffs": ffs_h, "fbr": fbr_h,
    }

    in_maps = []
    for c in range(NC):
        b = c // 4
        sl = (c % 4) * SL
        m = dict(shared)
        m["x"] = x[b]
        m["xq"] = x[b, sl:sl + SL]
        m["csq"] = mk_cs(np.arange(sl, sl + SL))
        in_maps.append(m)

    nc = _get_nc()
    res = run_bass_kernel_spmd(nc, in_maps, core_ids=list(range(NC)))
    global LAST_RES
    LAST_RES = res
    out = np.empty((B, S, DM), np.float32)
    for c in range(NC):
        b = c // 4
        sl = (c % 4) * SL
        out[b, sl:sl + SL] = np.asarray(res.results[c]["y"], np.float32)
    return out


# version 04
# speedup vs baseline: 1.0679x; 1.0679x over previous
import numpy as np
import ml_dtypes
import concourse.bass as bass
import concourse.mybir as mybir
from concourse.bass_utils import run_bass_kernel_spmd

F32 = mybir.dt.float32
BF16 = mybir.dt.bfloat16
AF = mybir.ActivationFunctionType
ALU = mybir.AluOpType
BF = ml_dtypes.bfloat16

B, S, DM, H, HD = 2, 8192, 1024, 16, 64
NC = 8
SL = S // 4
NP = H // 2
NB = S // 128
NQB = SL // 128
NCH = S // 128
QW = 512
NQW = SL // QW
GRP = [3] * 21 + [1]
NGR = len(GRP)
SHIFT = 4.0
EPS = 1e-5
KVC = 257
RBT = NB + NQB
NSLAB = RBT // 8

C_SHIFT, C_SCL, C_ONE, C_ZERO, C_NHALF, C_EPS = 0, 1, 2, 3, 4, 5

ENGS = ("sp", "pe", "act", "dve", "pool")
DEBUG_DUMPS = False
DEBUG_LN1_BYPASS = False
PHASES = "all"
LAST_RES = None


class Prog:

    def __init__(self):
        self.items = {e: [] for e in ENGS}

    def op(self, eng, fn, waits=(), inc=None, flush=False):
        self.items[eng].append((tuple(waits), fn, inc, flush))


def build_nc():
    nc = bass.Bass("TRN2", target_bir_lowering=False)

    d_x = nc.dram_tensor("x", [S, DM], F32, kind="ExternalInput").ap()
    d_xq = nc.dram_tensor("xq", [SL, DM], F32, kind="ExternalInput").ap()
    d_wkv = nc.dram_tensor("wkv", [8, 128, NP * KVC], F32, kind="ExternalInput").ap()
    d_wq = nc.dram_tensor("wq", [8, 128, NP * 128], F32, kind="ExternalInput").ap()
    d_bkv = nc.dram_tensor("bkv", [1, NP * KVC], F32, kind="ExternalInput").ap()
    d_bq = nc.dram_tensor("bq", [1, NP * 128], F32, kind="ExternalInput").ap()
    d_csk = nc.dram_tensor("csk", [128, 2 * NB * 64], F32, kind="ExternalInput").ap()
    d_csq = nc.dram_tensor("csq", [128, 2 * NQB * 64], F32, kind="ExternalInput").ap()
    d_cb = nc.dram_tensor("cbv", [128, 8], F32, kind="ExternalInput").ap()
    d_eyeb = nc.dram_tensor("eyeb", [128, 128], BF16, kind="ExternalInput").ap()
    d_eyef = nc.dram_tensor("eyef", [128, 128], F32, kind="ExternalInput").ap()
    d_win = nc.dram_tensor("ffw_in", [8, 128, 64], F32, kind="ExternalInput").ap()
    d_wout = nc.dram_tensor("ffw_out", [64, DM], F32, kind="ExternalInput").ap()
    d_ffs = nc.dram_tensor("ffs", [64, 50], F32, kind="ExternalInput").ap()
    d_fbr = nc.dram_tensor("fbr", [1, DM + 64], F32, kind="ExternalInput").ap()

    d_xlt = nc.dram_tensor("xlt", [8, 128, S], BF16, kind="Internal").ap()
    d_xltq = nc.dram_tensor("xltq", [8, 128, SL], BF16, kind="Internal").ap()
    d_osc = nc.dram_tensor("osc", [SL, DM], F32, kind="Internal").ap()
    d_x1sc = nc.dram_tensor("x1sc", [SL, DM], F32, kind="Internal").ap()
    d_y = nc.dram_tensor("y", [SL, DM], F32, kind="ExternalOutput").ap()

    A = nc.alloc_sbuf_tensor
    ST = A("ST", [128, 24], F32)
    MV = A("MV", [128, 4], F32)
    T1 = A("T1", [128, 1], F32)
    RS = A("RS", [128, 2], F32)
    BKV = A("BKV", [1, NP * KVC], BF16)
    BQ = A("BQ", [1, NP * 128], BF16)
    ONESR = A("ONESR", [1, 128], BF16)
    FFNW = A("FFNW", [128, 8 * 64], BF16)
    WOUT = A("WOUT", [64, DM], BF16)
    FFS = A("FFS", [64, 50], F32)
    BIN = A("BIN", [1, 64], BF16)
    BOUT = A("BOUT", [1, DM], BF16)
    CB = A("CB", [128, 8], F32)
    EYEB = A("EYEB", [128, 128], BF16)
    EYEF = A("EYEF", [128, 128], F32)
    KT = A("KT", [128, 2 * S], BF16)
    QT = A("QT", [128, 2 * SL], BF16)
    VAU = A("VAU", [128, 2 * NCH * 129], BF16)
    TRS = A("TRS", [128, 256], F32)
    KTOK = A("KTOK", [128, 8 * 128], BF16)
    PTA = A("PTA", [128, 2 * 1536], BF16)
    PTB = A("PTB", [128, 2 * 1024], BF16)
    OEA = A("OEA", [128, 512], F32)
    OEB = A("OEB", [128, 512], F32)
    RD = A("RD", [128, 8], F32)
    OTD = A("OTD", [128, 256], F32)
    OTD2 = A("OTD2", [128, 256], F32)
    SEV = A("SEV", [64, 128], F32)
    XB = A("XB", [128, 2 * 8192], BF16)
    XW = A("XW", [128, 3 * DM], F32)
    XLTS = A("XLTS", [128, 4 * 1024], BF16)
    WST = A("WST", [128, 8 * KVC], F32)
    CSB = A("CSB", [128, 2 * 1024], F32)
    XLN = A("XLN", [128, 2 * DM], BF16)
    WKVP = A("WKVP", [128, 8 * KVC], BF16)
    WQP = A("WQP", [128, 8 * 128], BF16)
    r1base = nc.lookup_mloc(XB).addr
    _off = [r1base]

    def AO(name, shape, dtype):
        nbytes = shape[-1] * (4 if dtype == F32 else 2)
        nbytes = (nbytes + 31) // 32 * 32
        t = nc.alloc_sbuf_tensor_at(name, shape, dtype, offset=_off[0])
        _off[0] += nbytes
        return t

    OSCT = AO("OSCT", [128, 2 * DM], F32)
    XQT = AO("XQT", [128, 2 * DM], F32)
    X1 = AO("X1", [128, 2 * DM], F32)
    TMP3 = AO("TMP3", [128, DM], F32)
    X1N = AO("X1N", [128, DM], BF16)
    X1NT = AO("X1NT", [128, DM], BF16)
    TM3 = AO("TM3", [64, 1024], F32)
    G3 = AO("G3", [64, 1024], F32)
    GB3 = AO("GB3", [64, 1024], BF16)
    OUTS = AO("OUTS", [128, DM], F32)
    YT = AO("YT", [128, DM], F32)
    MVS3 = AO("MVS3", [128, 16], F32)
    ST3 = AO("ST3", [128, 12], F32)
    MV3 = AO("MV3", [128, 2], F32)
    RSS3 = AO("RSS3", [128, 16], F32)
    SALL = AO("SALL", [64, 1024], F32)
    assert _off[0] <= r1base + 78 * 1024, "phase-3 overlay exceeds R1 region"

    PX = nc.alloc_psum_tensor("PX", [128, 1536], F32)
    PY = nc.alloc_psum_tensor("PY", [128, 1536], F32)
    POA = nc.alloc_psum_tensor("POA", [128, 512], F32)
    POB = nc.alloc_psum_tensor("POB", [128, 512], F32)

    def pslot(s):
        if s < 3:
            return PX[:, s * 512:(s + 1) * 512]
        return PY[:, (s - 3) * 512:(s - 2) * 512]

    sem = {}
    for name in ("sXL sST sSTA sRS sXA sXT sXE sXWR sWL sWC "
                 "sXBL sCSL sPJ sRP sKTB sKTE sRC "
                 "sSA sSB sEA sEB sAVA sAVB sOEA sOEB sTRA sTRB sDVA sDVB sOSW "
                 "sOSR sXQL s3ST s3RS s3X1 s3A1 s3S2 s3N s3T sX1E s3S s3M s3O s3OE s3TT s3F sYW sDBG").split():
        sem[name] = nc.alloc_semaphore(name)

    P = Prog()

    wl = [0]
    wc = [0]

    def load(dst, src, eng="sp"):
        P.op(eng, lambda e, d=dst, s=src: e.dma_start(d, s), inc=("sWL", 16))
        wl[0] += 16

    load(CB[:], d_cb)
    load(EYEB[:], d_eyeb)
    load(EYEF[:], d_eyef)
    load(FFS[:], d_ffs)

    P.op("dve", lambda e: e.memset(ONESR[:], 1.0), inc=("sWC", 1)); wc[0] += 1

    P.op("sp", lambda e: e.dma_start(WST[0:1, 0:NP * KVC], d_bkv),
         waits=[("sWC", wc[0])], inc=("sWL", 16)); wl[0] += 16
    P.op("dve", lambda e: e.tensor_copy(BKV[:], WST[0:1, 0:NP * KVC]),
         waits=[("sWL", wl[0])], inc=("sWC", 1)); wc[0] += 1
    P.op("sp", lambda e: e.dma_start(WST[0:1, 0:NP * 128], d_bq),
         waits=[("sWC", wc[0])], inc=("sWL", 16)); wl[0] += 16
    P.op("dve", lambda e: e.tensor_copy(BQ[:], WST[0:1, 0:NP * 128]),
         waits=[("sWL", wl[0])], inc=("sWC", 1)); wc[0] += 1
    for dc in range(8):
        P.op("sp", lambda e, dc=dc: e.dma_start(WST[:, 0:64], d_win[dc]),
             waits=[("sWC", wc[0])], inc=("sWL", 16)); wl[0] += 16
        P.op("dve", lambda e, dc=dc: e.tensor_copy(FFNW[:, dc * 64:(dc + 1) * 64], WST[:, 0:64]),
             waits=[("sWL", wl[0])], inc=("sWC", 1)); wc[0] += 1
    P.op("sp", lambda e: e.dma_start(WST[0:64, 0:DM], d_wout),
         waits=[("sWC", wc[0])], inc=("sWL", 16)); wl[0] += 16
    P.op("dve", lambda e: e.tensor_copy(WOUT[:], WST[0:64, 0:DM]),
         waits=[("sWL", wl[0])], inc=("sWC", 1)); wc[0] += 1
    P.op("sp", lambda e: e.dma_start(WST[0:1, 0:DM + 64], d_fbr),
         waits=[("sWC", wc[0])], inc=("sWL", 16)); wl[0] += 16
    P.op("dve", lambda e: e.tensor_copy(BIN[:], WST[0:1, 0:64]),
         waits=[("sWL", wl[0])], inc=("sWC", 1)); wc[0] += 1
    P.op("dve", lambda e: e.tensor_copy(BOUT[:], WST[0:1, 64:64 + DM]),
         inc=("sWC", 1)); wc[0] += 1
    W_ALL = wl[0]
    WC_ALL = wc[0]

    def ph1():
        nxl = [0]
        NT = NB + NQB

        def src_of(t):
            isq = t >= NB
            tt = t - NB if isq else t
            return (d_xq if isq else d_x)[tt * 128:(tt + 1) * 128, :], isq, tt

        def emit_stats(t):
            src, isq, tt = src_of(t)
            b3 = t % 3
            buf = t % 2
            w = [("sSTA", t - 2)] if t >= 3 else []
            if t == 0:
                w.append(("sWL", W_ALL))
            P.op("sp", lambda e, s_=src, b=b3: e.dma_start(XW[:, b * DM:(b + 1) * DM], s_),
                 waits=w, inc=("sXL", 16)); nxl[0] += 16
            w2 = [("sXL", nxl[0])]
            if t >= 2:
                w2.append(("sRS", t - 1))
            P.op("dve", lambda e, b=buf, b3=b3: e.bn_stats(ST[:, b * 6:(b + 1) * 6], XW[:, b3 * DM:b3 * DM + 512]),
                 waits=w2)
            P.op("dve", lambda e, b=buf, b3=b3: e.bn_stats(ST[:, 12 + b * 6:12 + (b + 1) * 6],
                                                    XW[:, b3 * DM + 512:b3 * DM + 1024]))

        def emit_aggr(t):
            buf = t % 2
            ap_in = ST[:].rearrange("p (h x) -> p h x", h=2)[:, :, 6 * buf:6 * buf + 6]
            if t == 0:
                P.op("dve", lambda e, a=ap_in: (e.bn_aggr(MV[:, 0:2], a), e.drain())[0],
                     inc=("sST", 1), flush=True)
            else:
                P.op("dve", lambda e, b=buf, a=ap_in: e.bn_aggr(MV[:, 2 * b:2 * b + 2], a))

        def emit_rstd(t):
            buf = t % 2
            wl_ = [("sSTA", t)] if t >= 1 else [("sST", 1)]
            P.op("act", lambda e, b=buf: e.activation(T1[:], MV[:, 2 * b + 1:2 * b + 2], AF.Ln,
                                                      bias=CB[:, C_EPS:C_EPS + 1], scale=CB[:, C_ONE:C_ONE + 1]),
                 waits=wl_)
            P.op("act", lambda e, b=buf: e.activation(RS[:, b:b + 1], T1[:], AF.Exp,
                                                      bias=CB[:, C_ZERO:C_ZERO + 1],
                                                      scale=CB[:, C_NHALF:C_NHALF + 1]),
                 inc=("sRS", 1))

        def emit_apply(t):
            buf = t % 2
            b3 = t % 3
            w = [("sRS", t + 1)]
            if t >= 2:
                w.append(("sXT", t - 1))
            P.op("dve", lambda e, b=buf, b3=b3: e.tensor_scalar(
                XLN[:, b * DM:(b + 1) * DM], XW[:, b3 * DM:(b3 + 1) * DM],
                MV[:, 2 * b:2 * b + 1], RS[:, b:b + 1], ALU.subtract, ALU.mult),
                waits=w, inc=("sSTA", 1))

        def emit_tr_evac(t):
            _, isq, tt = src_of(t)
            buf = t % 2
            dst_dram = d_xltq if isq else d_xlt
            slab = tt // 8
            PXb = PX[:, 0:512].bitcast(BF16)

            def tr(e, b=buf):
                last = None
                for j in range(8):
                    last = e.transpose(PXb[:, j * 128:(j + 1) * 128],
                                       XLN[:, b * DM + j * 128:b * DM + (j + 1) * 128], EYEB[:])
                return last
            P.op("pe", tr, waits=[("sSTA", t + 1), ("sXE", t)], inc=("sXT", 1))
            w = [("sXT", t + 1)]
            if t >= 4:
                w.append(("sXWR", 16 * (t // 4)))

            def ev(e, tt=tt):
                src_ap = PXb[:, 0:1024].rearrange("p (j u) -> p j u", j=8)
                dst_ap = XLTS[:].rearrange("p (j u) -> p j u", j=8)[:, :, (tt % 4) * 128:(tt % 4 + 1) * 128]
                return e.tensor_copy(dst_ap, src_ap)
            P.op("dve", ev, waits=w, inc=("sXE", 1), flush=True)
            if t % 4 == 3:
                sl4 = tt // 4
                def wr(e, dd=dst_dram, sl=sl4):
                    dst = dd[:, :, sl * 512:(sl + 1) * 512].rearrange("j p t -> p j t")
                    src = XLTS[:].rearrange("p (j u) -> p j u", j=8)
                    return e.dma_start(dst, src)
                P.op("pool", wr, waits=[("sXE", t + 1)], inc=("sXWR", 16))

        for t in range(NT + 2):
            if t < NT:
                emit_stats(t)
            if 1 <= t <= NT:
                emit_aggr(t - 1)
                emit_rstd(t - 1)
            if 2 <= t <= NT + 1:
                emit_apply(t - 2)
                emit_tr_evac(t - 2)
    ph1()
    XWR_ALL = 16 * ((NB + NQB) // 4)

    GRPB = [2] * 32
    NGRB = len(GRPB)
    slab_hist = {0: [], 1: []}
    SPARE = PY[:, 1024:1536]

    def emit_a1(i):
        buf = i % 2
        qwi = i // 4
        qq7 = NP * NQW - NQW + qwi
        w = [("sOSW", 16 * 8 * (qq7 + 1))]
        if i >= 2:
            w.append(("s3X1", 16 * (i - 1)))
        P.op("sp", lambda e, i=i, b=buf: e.dma_start(OSCT[:, b * DM:(b + 1) * DM],
                                                     d_osc[i * 128:(i + 1) * 128, :]),
             waits=w, inc=("sOSR", 16))
        wx = [("s3X1", 16 * (i - 1))] if i >= 2 else []
        P.op("sp", lambda e, i=i, b=buf: e.dma_start(XQT[:, b * DM:(b + 1) * DM],
                                                     d_xq[i * 128:(i + 1) * 128, :]),
             waits=wx, inc=("sXQL", 16))
        P.op("dve", lambda e, b=buf: e.bn_stats(ST3[:, 0:6], OSCT[:, b * DM:b * DM + 512]),
             waits=[("sOSR", 16 * (i + 1)), ("s3RS", 2 * i)])
        P.op("dve", lambda e, b=buf: e.bn_stats(ST3[:, 6:12], OSCT[:, b * DM + 512:b * DM + 1024]))
        P.op("dve", lambda e: (e.drain(), e.bn_aggr(MV3[:], ST3[:]))[1], inc=("s3ST", 1), flush=True)
        P.op("act", lambda e: e.activation(T1[:], MV3[:, 1:2], AF.Ln,
                                           bias=CB[:, C_EPS:C_EPS + 1], scale=CB[:, C_ONE:C_ONE + 1]),
             waits=[("s3ST", 2 * i + 1)])
        P.op("act", lambda e: e.activation(RS[:, 0:1], T1[:], AF.Exp,
                                           bias=CB[:, C_ZERO:C_ZERO + 1], scale=CB[:, C_NHALF:C_NHALF + 1]),
             inc=("s3RS", 1), flush=True)
        P.op("dve", lambda e, b=buf: e.tensor_scalar(TMP3[:], OSCT[:, b * DM:(b + 1) * DM],
                                                     MV3[:, 0:1], RS[:, 0:1], ALU.subtract, ALU.mult),
             waits=[("s3RS", 2 * i + 1), ("sXQL", 16 * (i + 1))])
        P.op("dve", lambda e, b=buf: e.tensor_add(X1[:, b * DM:(b + 1) * DM], TMP3[:],
                                                  XQT[:, b * DM:(b + 1) * DM]),
             inc=("s3A1", 1), flush=True)
        P.op("dve", lambda e, b=buf: e.bn_stats(ST3[:, 0:6], X1[:, b * DM:b * DM + 512]))
        P.op("dve", lambda e, b=buf: e.bn_stats(ST3[:, 6:12], X1[:, b * DM + 512:b * DM + 1024]))
        P.op("dve", lambda e: (e.drain(), e.bn_aggr(MV3[:], ST3[:]))[1], inc=("s3ST", 1), flush=True)
        P.op("act", lambda e: e.activation(T1[:], MV3[:, 1:2], AF.Ln,
                                           bias=CB[:, C_EPS:C_EPS + 1], scale=CB[:, C_ONE:C_ONE + 1]),
             waits=[("s3ST", 2 * i + 2)])
        P.op("act", lambda e, i=i: e.activation(RSS3[:, i:i + 1], T1[:], AF.Exp,
                                                bias=CB[:, C_ZERO:C_ZERO + 1], scale=CB[:, C_NHALF:C_NHALF + 1]),
             inc=("s3RS", 1), flush=True)
        P.op("dve", lambda e, i=i: e.tensor_copy(MVS3[:, i:i + 1], MV3[:, 0:1]),
             waits=[("s3RS", 2 * i + 2)])
        P.op("pool", lambda e, i=i, b=buf: e.dma_start(d_x1sc[i * 128:(i + 1) * 128, :],
                                                       X1[:, b * DM:(b + 1) * DM]),
             waits=[("s3A1", i + 1)], inc=("s3X1", 16))



    def rc_units(rp):
        ring = (rp == 0)
        rb0 = rp * RBT
        units = []

        def w_unit():
            src_kv = d_wkv[:, :, rp * KVC:(rp + 1) * KVC].rearrange("j p c -> p j c")
            P.op("sp", lambda e, s_=src_kv: e.dma_start(
                WST[:].rearrange("p (j c) -> p j c", j=8), s_),
                 waits=[("sWC", wc[0])], inc=("sWL", 16)); wl[0] += 16
            P.op("dve", lambda e: e.tensor_copy(WKVP[:], WST[:]),
                 waits=[("sWL", wl[0]), ("sPJ", rp * RBT)], inc=("sWC", 1)); wc[0] += 1
            src_q = d_wq[:, :, rp * 128:(rp + 1) * 128].rearrange("j p c -> p j c")
            P.op("sp", lambda e, s_=src_q: e.dma_start(
                WST[:, 0:1024].rearrange("p (j c) -> p j c", j=8), s_),
                 waits=[("sWC", wc[0])], inc=("sWL", 16)); wl[0] += 16
            P.op("dve", lambda e: e.tensor_copy(WQP[:], WST[:, 0:1024]),
                 waits=[("sWL", wl[0]), ("sPJ", rp * RBT)], inc=("sWC", 1)); wc[0] += 1
        units.append(w_unit)
        wc_pair = wc[0] + 2

        def slab_unit(si):
            def f(si=si):
                buf = (rp * NSLAB + si) % 2
                isq = si >= 8
                w = []
                if slab_hist[buf]:
                    w.append(("sPJ", slab_hist[buf][-1]))
                if isq:
                    qsl = si - 8
                    w.append(("sXWR", 16 * min(16 + 2 * (qsl + 1), XWR_ALL // 16)))
                    src = d_xltq[:, :, qsl * 1024:(qsl + 1) * 1024].rearrange("j p t -> p j t")
                else:
                    w.append(("sXWR", 16 * min(2 * (si + 1), XWR_ALL // 16)))
                    src = d_xlt[:, :, si * 1024:(si + 1) * 1024].rearrange("j p t -> p j t")
                dst = XB[:, buf * 8192:(buf + 1) * 8192].rearrange("p (j t) -> p j t", j=8)
                P.op("sp", lambda e, d=dst, s=src: e.dma_start(d, s), waits=w, inc=("sXBL", 16))
                slab_hist[buf].append(rb0 + min((si + 1) * 8, RBT))
                if isq:
                    qsl = si - 8
                    c_src = d_csq[:, qsl * 512:(qsl + 1) * 512]
                    s_src = d_csq[:, NQB * 64 + qsl * 512: NQB * 64 + (qsl + 1) * 512]
                else:
                    c_src = d_csk[:, si * 512:(si + 1) * 512]
                    s_src = d_csk[:, NB * 64 + si * 512: NB * 64 + (si + 1) * 512]
                wcs = [("sRP", slab_hist[buf][-2])] if len(slab_hist[buf]) >= 2 else []
                P.op("sp", lambda e, s_=c_src, b=buf: e.dma_start(CSB[:, b * 1024:b * 1024 + 512], s_),
                     waits=wcs, inc=("sCSL", 16))
                P.op("sp", lambda e, s_=s_src, b=buf: e.dma_start(CSB[:, b * 1024 + 512:b * 1024 + 1024], s_),
                     waits=[], inc=("sCSL", 16))
            return f

        def block_unit(ib):
            def f(ib=ib):
                rb = rb0 + ib
                isq = ib >= NB
                si = ib // 8
                buf = (rp * NSLAB + si) % 2
                bs = ib % 8
                w = [("sXBL", 16 * (rp * NSLAB + si + 1)), ("sCSL", 32 * (rp * NSLAB + si + 1))]
                if ring:
                    if rb >= 5:
                        w.append(("sRP", rb - 4))
                else:
                    w.append(("sRP", rb))
                    if ib % 8 == 0:
                        w.append(("sKTE", rb // 8))
                if ib == 0:
                    w.append(("sWC", wc_pair))
                slot = pslot(1 + rb % 5) if ring else SPARE
                ncols = 128 if isq else KVC

                def proj(e, rp=rp, buf=buf, bs=bs, isq=isq, slot=slot, ncols=ncols):
                    last = None
                    for dc in range(8):
                        lhsT = XB[:, buf * 8192 + dc * 1024 + bs * 128: buf * 8192 + dc * 1024 + (bs + 1) * 128]
                        rhs = WQP[:, dc * 128:(dc + 1) * 128] if isq else WKVP[:, dc * KVC:(dc + 1) * KVC]
                        last = e.matmul(slot[:, 0:ncols], lhsT, rhs, start=(dc == 0), stop=False)
                    bias = BQ[:, rp * 128:(rp + 1) * 128] if isq else BKV[:, rp * KVC:(rp + 1) * KVC]
                    return e.matmul(slot[:, 0:ncols], ONESR[:], bias, start=False, stop=True)
                P.op("pe", proj, waits=w, inc=("sPJ", 1))

                w = [("sPJ", rb + 1)]
                if rb >= 8:
                    w.append(("sKTB", rb // 8))
                blk = ib if not isq else (ib - NB)
                co = buf * 1024 + (ib % 8) * 64
                par = rp % 2

                def rope2(e, slot=slot, co=co, kslot=rb % 8, isq=isq, blk=blk, par=par):
                    xr = slot[:, 0:128:2]
                    xi = slot[:, 1:128:2]
                    c_ap = CSB[:, co:co + 64]
                    s_ap = CSB[:, co + 512:co + 512 + 64]
                    e.tensor_mul(TRS[:, 0:64], xr, c_ap)
                    e.tensor_mul(TRS[:, 64:128], xi, s_ap)
                    e.tensor_mul(TRS[:, 128:192], xr, s_ap)
                    e.tensor_mul(TRS[:, 192:256], xi, c_ap)
                    kt = KTOK[:, kslot * 128:(kslot + 1) * 128]
                    e.tensor_sub(kt[:, 0:128:2], TRS[:, 0:64], TRS[:, 64:128])
                    last = e.tensor_add(kt[:, 1:128:2], TRS[:, 128:192], TRS[:, 192:256])
                    if not isq:
                        last = e.tensor_copy(VAU[:, par * NCH * 129 + blk * 129: par * NCH * 129 + (blk + 1) * 129],
                                             slot[:, 128:KVC])
                    return last
                P.op("dve", rope2, waits=w, inc=("sRP", 1))
            return f

        def batch_unit(ib):
            def f(ib=ib):
                rb = rb0 + ib
                isq = ib >= NB
                blk = ib if not isq else (ib - NB)
                par = rp % 2
                bpsum = POA[:].bitcast(BF16) if ring else SPARE.bitcast(BF16)

                def trb(e, bpsum=bpsum):
                    last = None
                    for j in range(8):
                        last = e.transpose(bpsum[:, j * 128:(j + 1) * 128],
                                           KTOK[:, j * 128:(j + 1) * 128], EYEB[:])
                    return last
                P.op("pe", trb, waits=[("sRP", rb + 1), ("sKTE", rb // 8)], inc=("sKTB", 1))
                dstT = QT if isq else KT
                off = (par * (SL if isq else S)) + (blk - 7) * 128
                P.op("dve", lambda e, d=dstT, off=off, bpsum=bpsum: e.tensor_copy(
                    d[:, off:off + 1024], bpsum[:, 0:1024]),
                    waits=[("sKTB", rb // 8 + 1)], inc=("sKTE", 1))
            return f

        for si in range(NSLAB):
            units.append(slab_unit(si))
            for ib in range(si * 8, si * 8 + 8):
                units.append(block_unit(ib))
            units.append(batch_unit(si * 8 + 7))

        def done_unit():
            P.op("dve", lambda e: e.engine_nop(), waits=[("sKTE", (rp + 1) * NSLAB)], inc=("sRC", 1))
        units.append(done_unit)
        return units

    def _avu(p, qq, glist, g, u, PT, ptstride, PO, voff, sE, sAV):
        glen = glist[g]
        cbase = sum(glist[:g])
        buf = u % 2
        waits = [(sE, u + 1)]
        if g == 0 and qq > 0:
            waits.append(("sDVA", qq))
        par = p % 2

        def av(e, glen=glen, cbase=cbase, buf=buf, g=g, voff=voff, par=par):
            last = None
            for j in range(glen):
                c = cbase + j
                last = e.matmul(PO[0:65, 0:QW],
                                VAU[:, par * NCH * 129 + c * 129 + voff: par * NCH * 129 + c * 129 + voff + 65],
                                PT[:, buf * ptstride + j * 512: buf * ptstride + (j + 1) * 512],
                                start=(g == 0 and j == 0), stop=(g == len(glist) - 1 and j == glen - 1))
            return last
        P.op("pe", av, waits=waits, inc=(sAV, 1))

    def emit_pair(p):
        par = p % 2
        units = rc_units(p + 1) if p + 1 < NP else []
        uidx = [0]
        tot_slots = NQW * NGRB

        def pump_rc(s_):
            while uidx[0] < len(units) and uidx[0] * tot_slots < (s_ + 1) * len(units):
                units[uidx[0]]()
                uidx[0] += 1

        for wq_i in range(NQW):
            ubA = (p * NQW + wq_i) * NGR
            ubB = (p * NQW + wq_i) * NGRB
            qq = p * NQW + wq_i
            qoff = wq_i * QW
            aidx = [0]

            def sc_A(g):
                u = ubA + g
                glen = GRP[g]
                cbase = sum(GRP[:g])
                w = []
                if u > 0:
                    w.append(("sEA", u))
                if g == 0 and wq_i == 0:
                    w.append(("sRC", p + 1))

                def sa(e, glen=glen, cbase=cbase, qoff=qoff, par=par):
                    last = None
                    for j in range(glen):
                        c = cbase + j
                        last = e.matmul(PX[:, j * 512:(j + 1) * 512],
                                        KT[0:64, par * S + c * 128: par * S + (c + 1) * 128],
                                        QT[0:64, par * SL + qoff: par * SL + qoff + QW],
                                        start=True, stop=True)
                    return last
                P.op("pe", sa, waits=w, inc=("sSA", 1))
                fd = glen * 512
                w = [("sSA", u + 1)]
                if u >= 2:
                    w.append(("sAVA", u - 1))
                P.op("act", lambda e, fd=fd, buf=u % 2: e.activation(
                    PTA[:, buf * 1536:buf * 1536 + fd], PX[:, 0:fd], AF.Exp,
                    bias=CB[:, C_SHIFT:C_SHIFT + 1], scale=CB[:, C_SCL:C_SCL + 1]),
                    waits=w, inc=("sEA", 1))

            def sc_B(k):
                u = ubB + k
                glen = GRPB[k]
                cbase = sum(GRPB[:k])
                w = []
                if u > 0:
                    w.append(("sEB", u))
                if k == 0 and wq_i == 0:
                    w.append(("sRC", p + 1))

                def sb(e, glen=glen, cbase=cbase, qoff=qoff, par=par):
                    last = None
                    for j in range(glen):
                        c = cbase + j
                        last = e.matmul(PY[:, j * 512:(j + 1) * 512],
                                        KT[64:128, par * S + c * 128: par * S + (c + 1) * 128],
                                        QT[64:128, par * SL + qoff: par * SL + qoff + QW],
                                        start=True, stop=True, tile_position=(64, 0))
                    return last
                P.op("pe", sb, waits=w, inc=("sSB", 1))
                w = [("sSB", u + 1)]
                if u >= 2:
                    w.append(("sAVB", u - 1))
                P.op("act", lambda e, buf=u % 2: e.activation(
                    PTB[:, buf * 1024:buf * 1024 + 1024], PY[:, 0:1024], AF.Exp,
                    bias=CB[:, C_SHIFT:C_SHIFT + 1], scale=CB[:, C_SCL:C_SCL + 1]),
                    waits=w, inc=("sEB", 1))

            for k in range(NGRB):
                sc_B(k)
                while aidx[0] * NGRB < (k + 1) * NGR:
                    a = aidx[0]
                    sc_A(a)
                    if a > 0:
                        _avu(p, qq, GRP, a - 1, ubA + a - 1, PTA, 1536, POA, 0, "sEA", "sAVA")
                    aidx[0] += 1
                if k > 0:
                    _avu(p, qq, GRPB, k - 1, ubB + k - 1, PTB, 1024, POB, 64, "sEB", "sAVB")
                pump_rc(wq_i * NGRB + k)
            _avu(p, qq, GRP, NGR - 1, ubA + NGR - 1, PTA, 1536, POA, 0, "sEA", "sAVA")
            _avu(p, qq, GRPB, NGRB - 1, ubB + NGRB - 1, PTB, 1024, POB, 64, "sEB", "sAVB")

            P.op("dve", lambda e: e.tensor_copy(OEA[0:65, :], POA[0:65, :]),
                 waits=[("sAVA", (qq + 1) * NGR)], inc=("sOEA", 1), flush=True)
            P.op("dve", lambda e: e.tensor_copy(OEB[0:65, :], POB[0:65, :]),
                 waits=[("sAVB", (qq + 1) * NGRB)], inc=("sOEB", 1), flush=True)

            def tra(e):
                last = None
                for j in range(4):
                    last = e.transpose(POA[:, j * 128:(j + 1) * 128], OEA[:, j * 128:(j + 1) * 128], EYEF[:])
                return last
            P.op("pe", tra, waits=[("sOEA", qq + 1)], inc=("sTRA", 1))

            def trb_(e):
                last = None
                for j in range(4):
                    last = e.transpose(POB[:, j * 128:(j + 1) * 128], OEB[:, j * 128:(j + 1) * 128], EYEF[:])
                return last
            P.op("pe", trb_, waits=[("sOEB", qq + 1)], inc=("sTRB", 1))

            w = [("sTRA", qq + 1), ("sTRB", qq + 1)]
            if qq > 0:
                w.append(("sOSW", 16 * 8 * qq))

            def dv(e):
                for j in range(4):
                    e.reciprocal(RD[:, j:j + 1], POA[:, j * 128 + 64:j * 128 + 65])
                for j in range(4):
                    e.reciprocal(RD[:, 4 + j:5 + j], POB[:, j * 128:j * 128 + 1])
                e.drain()
                last = None
                for j in range(4):
                    e.tensor_scalar(OTD[:, j * 64:(j + 1) * 64], POA[:, j * 128:j * 128 + 64],
                                    RD[:, j:j + 1], None, ALU.mult)
                    last = e.tensor_scalar(OTD2[:, j * 64:(j + 1) * 64], POB[:, j * 128 + 1:j * 128 + 65],
                                           RD[:, 4 + j:5 + j], None, ALU.mult)
                return last
            P.op("dve", dv, waits=w, inc=("sDVA", 1), flush=True)
            for j in range(4):
                dst = d_osc[qoff + j * 128: qoff + (j + 1) * 128, (2 * p) * 64:(2 * p + 1) * 64]
                P.op("pool", lambda e, d=dst, j=j: e.dma_start(d, OTD[:, j * 64:(j + 1) * 64]),
                     waits=[("sDVA", qq + 1)], inc=("sOSW", 16))
            for j in range(4):
                dst = d_osc[qoff + j * 128: qoff + (j + 1) * 128, (2 * p + 1) * 64:(2 * p + 2) * 64]
                P.op("pool", lambda e, d=dst, j=j: e.dma_start(d, OTD2[:, j * 64:(j + 1) * 64]),
                     waits=[("sDVA", qq + 1)], inc=("sOSW", 16))
            if p == NP - 1:
                for ti in range(wq_i * 4, wq_i * 4 + 4):
                    emit_a1(ti)
        pump_rc(tot_slots + 100)

    if PHASES == "all":
        for u in rc_units(0):
            u()
        for p in range(NP):
            emit_pair(p)

    for half in (range(2) if PHASES == "all" else []):
        for i in range(half * 8, half * 8 + 8):
            buf = i % 2
            w = [("s3X1", 16 * 16)]
            if i >= 2:
                w.append(("s3N", i - 1))
            P.op("sp", lambda e, i=i, b=buf: e.dma_start(X1[:, b * DM:(b + 1) * DM],
                                                         d_x1sc[i * 128:(i + 1) * 128, :]),
                 waits=w, inc=("sOSR", 16))
            wap = [("sOSR", 16 * (17 + i))]
            if i >= 1:
                wap.append(("s3T", i))
            P.op("dve", lambda e, i=i, b=buf: e.tensor_scalar(X1N[:], X1[:, b * DM:(b + 1) * DM],
                                                              MVS3[:, i:i + 1], RSS3[:, i:i + 1],
                                                              ALU.subtract, ALU.mult),
                 waits=wap, inc=None, flush=False)
            PXb = PX[:, 0:512].bitcast(BF16)
            def t3(e):
                last = None
                for j in range(8):
                    last = e.transpose(PXb[:, j * 128:(j + 1) * 128], X1N[:, j * 128:(j + 1) * 128], EYEB[:])
                return last
            wpe = [("sX1E", i)] if i >= 1 else [("sDVA", NP * NQW)]
            P.op("dve", lambda e: e.drain(), inc=("s3N", 1))
            P.op("pe", t3, waits=[("s3N", i + 1)] + wpe, inc=("s3T", 1))
            P.op("dve", lambda e: e.tensor_copy(X1NT[:], PXb[:, 0:1024]),
                 waits=[("s3T", i + 1)], inc=("sX1E", 1), flush=True)
            def smm(e):
                for dc in range(8):
                    e.matmul(PY[0:64, 0:128], FFNW[:, dc * 64:(dc + 1) * 64],
                             X1NT[:, dc * 128:(dc + 1) * 128], start=(dc == 0), stop=False)
                return e.matmul(PY[0:64, 0:128], BIN[:], ONESR[:, 0:128], start=False, stop=True)
            P.op("pe", smm, waits=[("sX1E", i + 1), ("s3S2", i)], inc=("s3S", 1))
            P.op("dve", lambda e, i=i: e.tensor_copy(SALL[:, (i % 8) * 128:(i % 8 + 1) * 128],
                                                     PY[0:64, 0:128]),
                 waits=[("s3S", i + 1)], inc=("s3S2", 1), flush=True)
        def mid(e, half=half):
            for h in range(16):
                e.tensor_scalar(TM3[:], SALL[:, 0:1024], FFS[0:64, h:h + 1], FFS[0:64, 16 + h:17 + h],
                                ALU.mult, ALU.add)
                e.tensor_scalar(TM3[:], TM3[:], CB[0:64, C_ZERO:C_ZERO + 1], None, ALU.max)
                if h == 0:
                    e.tensor_scalar(G3[:], TM3[:], FFS[0:64, 32:33], FFS[0:64, 48:49],
                                    ALU.mult, ALU.add)
                else:
                    e.scalar_tensor_tensor(G3[:], TM3[:], FFS[0:64, 32 + h:33 + h], G3[:],
                                           ALU.mult, ALU.add)
                e.drain()
            return e.tensor_copy(GB3[:], G3[:])
        wmid = [("s3S2", half * 8 + 8)]
        if half == 1:
            wmid.append(("s3O", 8))
        P.op("dve", mid, waits=wmid, inc=("s3M", 1), flush=True)
        for i in range(half * 8, half * 8 + 8):
            buf = i % 2
            w = [("s3M", half + 1)]
            if i >= 2:
                w.append(("s3F", i - 1))
            P.op("sp", lambda e, i=i, b=buf: e.dma_start(XQT[:, b * DM:(b + 1) * DM],
                                                         d_x1sc[i * 128:(i + 1) * 128, :]),
                 waits=w, inc=("sXQL", 16))
            def omm(e, i=i):
                last = None
                for dc in range(8):
                    ps = PX[:, 512 + dc * 128:512 + (dc + 1) * 128]
                    e.matmul(ps, WOUT[:, dc * 128:(dc + 1) * 128], GB3[:, (i % 8) * 128:(i % 8 + 1) * 128],
                             start=True, stop=False)
                    last = e.matmul(ps, BOUT[:, dc * 128:(dc + 1) * 128], ONESR[:, 0:128], start=False, stop=True)
                return last
            P.op("pe", omm, waits=[("s3M", half + 1), ("s3OE", i)], inc=("s3O", 1))
            P.op("dve", lambda e: e.tensor_copy(OUTS[:], PX[:, 512:1536]),
                 waits=[("s3O", i + 1)], inc=("s3OE", 1), flush=True)
            def t3b(e):
                last = None
                for dc in range(8):
                    last = e.transpose(PY[:, 512 + dc * 128:512 + (dc + 1) * 128],
                                       OUTS[:, dc * 128:(dc + 1) * 128], EYEF[:])
                return last
            P.op("pe", t3b, waits=[("s3OE", i + 1), ("s3F", i)], inc=("s3TT", 1))
            P.op("dve", lambda e: e.bn_stats(ST3[:, 0:6], PY[:, 512:1024]), waits=[("s3TT", i + 1)])
            P.op("dve", lambda e: e.bn_stats(ST3[:, 6:12], PY[:, 1024:1536]))
            P.op("dve", lambda e: (e.drain(), e.bn_aggr(MV3[:], ST3[:]))[1], inc=("s3ST", 1), flush=True)
            P.op("act", lambda e: e.activation(T1[:], MV3[:, 1:2], AF.Ln,
                                               bias=CB[:, C_EPS:C_EPS + 1], scale=CB[:, C_ONE:C_ONE + 1]),
                 waits=[("s3ST", 32 + i + 1)])
            P.op("act", lambda e: e.activation(RS[:, 0:1], T1[:], AF.Exp,
                                               bias=CB[:, C_ZERO:C_ZERO + 1], scale=CB[:, C_NHALF:C_NHALF + 1]),
                 inc=("s3RS", 1), flush=True)
            P.op("dve", lambda e: e.tensor_scalar(TMP3[:], PY[:, 512:1536], MV3[:, 0:1], RS[:, 0:1],
                                                  ALU.subtract, ALU.mult),
                 waits=[("s3RS", 32 + i + 1)])
            wyt = [("sYW", 16 * i)] if i >= 1 else []
            P.op("dve", lambda e, b=buf: e.tensor_add(YT[:], TMP3[:], XQT[:, b * DM:(b + 1) * DM]),
                 waits=wyt + [("sXQL", 16 * (17 + i))], inc=("s3F", 1), flush=True)
            P.op("pool", lambda e, i=i: e.dma_start(d_y[i * 128:(i + 1) * 128, :], YT[:]),
                 waits=[("s3F", i + 1)], inc=("sYW", 16))

    FINAL_WAIT = ("sYW", 16 * 16) if PHASES == "all" else ("sXWR", XWR_ALL)
    P.op("sp", lambda e: e.nop(), waits=[FINAL_WAIT])
    if DEBUG_DUMPS:
        d_dosc = nc.dram_tensor("dbg_osc", [SL, DM], F32, kind="ExternalOutput").ap()
        d_dxlt = nc.dram_tensor("dbg_xlt", [8, 128, 1024], BF16, kind="ExternalOutput").ap()
        d_dxlq = nc.dram_tensor("dbg_xltq", [8, 128, 1024], BF16, kind="ExternalOutput").ap()
        d_dkt = nc.dram_tensor("dbg_kt", [128, S], BF16, kind="ExternalOutput").ap()
        d_dqt = nc.dram_tensor("dbg_qt", [128, SL], BF16, kind="ExternalOutput").ap()
        d_dva = nc.dram_tensor("dbg_vau", [128, 10 * 129], BF16, kind="ExternalOutput").ap()
        P.op("sp", lambda e: e.dma_start(d_dosc, d_osc[:]), waits=[FINAL_WAIT], inc=("sDBG", 16))
        P.op("sp", lambda e: e.dma_start(d_dxlt, d_xlt[:, :, 0:1024]), inc=("sDBG", 16))
        P.op("sp", lambda e: e.dma_start(d_dxlq, d_xltq[:, :, 0:1024]), inc=("sDBG", 16))
        P.op("sp", lambda e: e.dma_start(d_dkt, KT[:, S:2 * S]), inc=("sDBG", 16))
        P.op("sp", lambda e: e.dma_start(d_dqt, QT[:, SL:2 * SL]), inc=("sDBG", 16))
        P.op("sp", lambda e: e.dma_start(d_dva, VAU[:, NCH * 129:NCH * 129 + 10 * 129]), inc=("sDBG", 16))
        P.op("sp", lambda e: e.nop(), waits=[("sDBG", 96)])

    build_nc.last_prog = P
    with nc.Block() as block:
        def body_for(name):
            def body(eng):
                for waits, fn, inc, flush in P.items[name]:
                    for (s, v) in waits:
                        if v > 0:
                            eng.wait_ge(sem[s], v)
                    instr = fn(eng)
                    if inc is not None:
                        if flush:
                            eng.drain().then_inc(sem[inc[0]], inc[1])
                        else:
                            instr.then_inc(sem[inc[0]], inc[1])
            return body
        block.sync(body_for("sp"))
        block.tensor(body_for("pe"))
        block.scalar(body_for("act"))
        block.vector(body_for("dve"))
        block.gpsimd(body_for("pool"))

    return nc


def _emit_av(P, p, wq_i, g, u, PT, VAU, PO, voff, sE, sAV, sDV):
    glen = GRP[g]
    cbase = sum(GRP[:g])
    buf = u % 2
    qq = p * NQW + wq_i
    waits = [(sE, u + 1)]
    if g == 0 and qq > 0:
        waits.append((sDV, qq))
    def av(e, glen=glen, cbase=cbase, buf=buf, g=g, voff=voff):
        last = None
        for j in range(glen):
            c = cbase + j
            last = e.matmul(PO[0:65, 0:QW],
                            VAU[:, c * 129 + voff: c * 129 + voff + 65],
                            PT[:, buf * 1536 + j * 512: buf * 1536 + (j + 1) * 512],
                            start=(g == 0 and j == 0), stop=(g == NGR - 1 and j == glen - 1))
        return last
    P.op("pe", av, waits=waits, inc=(sAV, 1))


_NC_CACHE = {}


def _get_nc():
    if "nc" not in _NC_CACHE:
        _NC_CACHE["nc"] = build_nc()
    return _NC_CACHE["nc"]


def kernel(L0_attn_x, L0_attn_wq_w, L0_attn_wq_b, L0_attn_wk_w, L0_attn_wk_b,
           L0_attn_wv_w, L0_attn_wv_b, L0_ffn_Win, L0_ffn_bin_, L0_ffn_W1,
           L0_ffn_b1, L0_ffn_W2, L0_ffn_b2, L0_ffn_Wout, L0_ffn_bout):
    x = np.asarray(L0_attn_x, dtype=np.float32)
    wq = np.asarray(L0_attn_wq_w, np.float32)
    wk = np.asarray(L0_attn_wk_w, np.float32)
    wv = np.asarray(L0_attn_wv_w, np.float32)
    bq = np.asarray(L0_attn_wq_b, np.float32)
    bk = np.asarray(L0_attn_wk_b, np.float32)
    bv = np.asarray(L0_attn_wv_b, np.float32)

    wqT = np.ascontiguousarray(wq.T)
    wkT = np.ascontiguousarray(wk.T)
    wvT = np.ascontiguousarray(wv.T)

    wkv_h = np.zeros((8, 128, NP * KVC), np.float32)
    bkv_h = np.zeros((1, NP * KVC), np.float32)
    wq_h = np.zeros((8, 128, NP * 128), np.float32)
    bq_h = np.zeros((1, NP * 128), np.float32)
    for p in range(NP):
        hA, hB = 2 * p, 2 * p + 1
        colsK = slice(p * KVC, p * KVC + 128)
        colsVA = slice(p * KVC + 128, p * KVC + 192)
        colsVB = slice(p * KVC + 193, p * KVC + 257)
        for dc in range(8):
            din = slice(dc * 128, (dc + 1) * 128)
            wkv_h[dc, :, colsK] = wkT[din, hA * 64:hB * 64 + 64]
            wkv_h[dc, :, colsVA] = wvT[din, hA * 64:(hA + 1) * 64]
            wkv_h[dc, :, colsVB] = wvT[din, hB * 64:(hB + 1) * 64]
            wq_h[dc, :, p * 128:(p + 1) * 128] = wqT[din, hA * 64:hB * 64 + 64]
        bkv_h[0, colsK] = bk[hA * 64:hB * 64 + 64]
        bkv_h[0, colsVA] = bv[hA * 64:(hA + 1) * 64]
        bkv_h[0, p * KVC + 192] = 1.0
        bkv_h[0, colsVB] = bv[hB * 64:(hB + 1) * 64]
        bq_h[0, p * 128:(p + 1) * 128] = bq[hA * 64:hB * 64 + 64]

    theta = (10000.0 ** (-(np.arange(32, dtype=np.float64) / 32.0))).astype(np.float64)
    pos = np.arange(S, dtype=np.float64)
    ang = np.outer(pos, theta)
    cosf = np.cos(ang).astype(np.float32)
    sinf = np.sin(ang).astype(np.float32)

    def mk_cs(tokens):
        nb = len(tokens) // 128
        c = np.zeros((128, nb * 64), np.float32)
        s = np.zeros((128, nb * 64), np.float32)
        for b in range(nb):
            tt = tokens[b * 128:(b + 1) * 128]
            cc = cosf[tt]
            ss = sinf[tt]
            c[:, b * 64:b * 64 + 32] = cc
            c[:, b * 64 + 32:b * 64 + 64] = cc
            s[:, b * 64:b * 64 + 32] = ss
            s[:, b * 64 + 32:b * 64 + 64] = ss
        return np.concatenate([c, s], axis=1)

    csk_h = mk_cs(np.arange(S))

    cb_h = np.zeros((128, 8), np.float32)
    cb_h[:, C_SHIFT] = -SHIFT
    cb_h[:, C_SCL] = 1.0 / np.sqrt(HD)
    cb_h[:, C_ONE] = 1.0
    cb_h[:, C_ZERO] = 0.0
    cb_h[:, C_NHALF] = -0.5
    cb_h[:, C_EPS] = EPS

    eyeb_h = np.eye(128).astype(BF)
    eyef_h = np.eye(128).astype(np.float32)

    win_h = np.asarray(L0_ffn_Win, np.float32).reshape(8, 128, 64)
    wout_h = np.asarray(L0_ffn_Wout, np.float32)
    ffs_h = np.zeros((64, 50), np.float32)
    ffs_h[:, 0:16] = np.asarray(L0_ffn_W1, np.float32)
    ffs_h[:, 16:32] = np.asarray(L0_ffn_b1, np.float32)
    ffs_h[:, 32:48] = np.asarray(L0_ffn_W2, np.float32)
    ffs_h[:, 48] = np.asarray(L0_ffn_b2, np.float32)
    fbr_h = np.zeros((1, DM + 64), np.float32)
    fbr_h[0, 0:64] = np.asarray(L0_ffn_bin_, np.float32)
    fbr_h[0, 64:] = np.asarray(L0_ffn_bout, np.float32)

    shared = {
        "wkv": wkv_h, "wq": wq_h, "bkv": bkv_h, "bq": bq_h,
        "csk": csk_h, "cbv": cb_h, "eyeb": eyeb_h, "eyef": eyef_h,
        "ffw_in": win_h, "ffw_out": wout_h, "ffs": ffs_h, "fbr": fbr_h,
    }

    in_maps = []
    for c in range(NC):
        b = c // 4
        sl = (c % 4) * SL
        m = dict(shared)
        m["x"] = x[b]
        m["xq"] = x[b, sl:sl + SL]
        m["csq"] = mk_cs(np.arange(sl, sl + SL))
        in_maps.append(m)

    nc = _get_nc()
    res = run_bass_kernel_spmd(nc, in_maps, core_ids=list(range(NC)))
    global LAST_RES
    LAST_RES = res
    out = np.empty((B, S, DM), np.float32)
    for c in range(NC):
        b = c // 4
        sl = (c % 4) * SL
        out[b, sl:sl + SL] = np.asarray(res.results[c]["y"], np.float32)
    return out
